# version 05 (final)
# baseline (speedup 1.0000x reference)
import numpy as np
from contextlib import ExitStack

import concourse.bass as bass
import concourse.mybir as mybir
from concourse.bass_utils import run_bass_kernel_spmd

F32 = mybir.dt.float32
BF16 = mybir.dt.bfloat16
AF = mybir.ActivationFunctionType
ALU = mybir.AluOpType
AX = mybir.AxisListType

D = 1024
NH = 16
HF = 4096
NE = 8
QT = 512
LN_EPS = 1e-5

_DBG = set()
_STOP = 99


class SemCtr:
    def __init__(self, nc, es, name):
        self.h = es.enter_context(nc.semaphore(name))
        self.n = 0

    def inc(self, instr, k=1):
        instr.then_inc(self.h, k)
        self.n += k
        return self.n

    def tok(self):
        return (self.h, self.n)


def W(eng, tok):
    if tok:
        eng.wait_ge(tok[0], tok[1])


class Ring:

    def __init__(self, depth):
        self.depth = depth
        self.hist = []

    def guard(self, eng):
        if len(self.hist) >= self.depth:
            W(eng, self.hist[-self.depth])

    def done(self, tok):
        self.hist.append(tok)


def build_nc(T=2048):
    if 30 <= _STOP <= 39:
        main_lvl, p2_lvl, s3_lvl, p5_n = 3, _STOP - 30, 99, 99
        p4_e = 0
    elif 41 <= _STOP <= 45:
        main_lvl, p2_lvl, s3_lvl, p5_n = 3, 3, _STOP - 40, 99
        p4_e = 0
    elif 50 <= _STOP <= 58:
        main_lvl, p2_lvl, s3_lvl, p5_n, p4_e = 5, 99, 99, 0, _STOP - 50 + 1
    elif 60 <= _STOP <= 77:
        main_lvl, p2_lvl, s3_lvl, p5_n, p4_e = 6, 99, 99, _STOP - 60, 99
    else:
        main_lvl, p2_lvl, s3_lvl, p5_n, p4_e = _STOP, 99, 99, 99, 99
    p4_e = p4_e if 50 <= _STOP <= 58 else (NE if main_lvl >= 5 else 0)
    assert T % QT == 0
    NQ = T // QT
    NT = T // 128

    nc = bass.Bass("TRN2", target_bir_lowering=False)
    dram = nc.dram_tensor

    x_in = dram("x_in", [T, D], F32, kind="ExternalInput")
    w_qkv = dram("w_qkv", [D, 3 * D], F32, kind="ExternalInput")
    w_proj = dram("w_proj", [D, D], F32, kind="ExternalInput")
    w_g = dram("w_g", [D, NE], F32, kind="ExternalInput")
    w1 = dram("w1", [NE, D, HF], F32, kind="ExternalInput")
    w2 = dram("w2", [NE, HF, D], F32, kind="ExternalInput")
    b2_in = dram("b2_in", [NE, D], F32, kind="ExternalInput")
    w_mod = dram("w_mod", [D, 6 * D], F32, kind="ExternalInput")
    b1_cols = dram("b1_cols", [128, NE * 32], F32, kind="ExternalInput")
    c_col = dram("c_col", [128, NE], F32, kind="ExternalInput")
    gam_bc_in = dram("gam_bc_in", [128, D], F32, kind="ExternalInput")
    bet_bc_in = dram("bet_bc_in", [128, D], F32, kind="ExternalInput")
    gf_bc_in = dram("gf_bc_in", [128, D], F32, kind="ExternalInput")
    bf_bc_in = dram("bf_bc_in", [128, D], F32, kind="ExternalInput")
    bproj_bc_in = dram("bproj_bc_in", [128, D], F32, kind="ExternalInput")
    bg_bc_in = dram("bg_bc_in", [128, NE], F32, kind="ExternalInput")
    perm64_in = dram("perm64_in", [128, 128], F32, kind="ExternalInput")
    ident_in = dram("ident_in", [128, 128], F32, kind="ExternalInput")

    y_out = dram("y_out", [T, D], F32, kind="ExternalOutput")
    x2_hbm = dram("x2_hbm", [T, D], F32, kind="Internal")
    rows_hbm = dram("rows_hbm", [8, D], F32, kind="Internal")

    dbg_t = {}
    for name, shape in [
        ("h1fm", [128, 8 * T]), ("qtm", [128, 4 * D]), ("ktm", [128, 4 * D]),
        ("vtm", [128, 4 * D]), ("qfm", [128, 8 * QT]), ("ptm", [128, 4 * D]),
        ("attnfm", [128, 8 * QT]), ("acc0", [128, NT * D]),
        ("h2fm", [128, 8 * T]), ("comb", [128, NT * 8]), ("mrows", [1, 6 * D]),
        ("a1", [128, D]), ("logits", [128, NT * 8]), ("accf", [128, NT * D]),
        ("g0", [128, 8 * T]),
    ]:
        if name in _DBG:
            dbg_t[name] = dram("dbg_" + name, shape, F32, kind="ExternalOutput")

    es = ExitStack()
    with es:
        sPE = SemCtr(nc, es, "sPE")
        sG = SemCtr(nc, es, "sG")
        sV = SemCtr(nc, es, "sV")
        sSC = SemCtr(nc, es, "sSC")
        dInit = SemCtr(nc, es, "dInit")
        dWM = [SemCtr(nc, es, "dWM%d" % i) for i in range(2)]
        dWQ = [SemCtr(nc, es, "dWQ%d" % i) for i in range(2)]
        dWP = [SemCtr(nc, es, "dWP%d" % i) for i in range(2)]
        dW = [SemCtr(nc, es, "dW%d" % i) for i in range(3)]
        dX = [SemCtr(nc, es, "dX%d" % i) for i in range(2)]
        dTs = SemCtr(nc, es, "dTs")
        dTa = SemCtr(nc, es, "dTa")
        dQs = SemCtr(nc, es, "dQs")
        dQa = SemCtr(nc, es, "dQa")
        dPs = SemCtr(nc, es, "dPs")
        dPa = SemCtr(nc, es, "dPa")
        dX2 = [SemCtr(nc, es, "dX2%d" % i) for i in range(2)]
        dXR = [SemCtr(nc, es, "dXR%d" % i) for i in range(2)]
        dOut = [SemCtr(nc, es, "dOut%d" % i) for i in range(2)]
        dRows = SemCtr(nc, es, "dRows")
        dBC = SemCtr(nc, es, "dBC")
        dDBG = SemCtr(nc, es, "dDBG")
        ALL_DMA = [dInit, *dWM, *dWQ, *dWP, *dW, *dX, dTs, dTa, dQs, dQa,
                   dPs, dPa, *dX2, *dXR, *dOut, dRows, dBC, dDBG]

        block = es.enter_context(nc.Block())
        SBT = lambda *a, **k: es.enter_context(nc.sbuf_tensor(*a, **k))

        def barrier_all():
            toks = [(sem.h, sem.n) for sem in [sPE, sG, sV, sSC] + ALL_DMA
                    if sem.n > 0]

            def mk(e, toks=toks):
                for tk in toks:
                    W(e, tk)
            block.sync(mk)
            block.scalar(mk)
            block.vector(mk)
            block.tensor(mk)
            block.gpsimd(mk)

        def dbg_dump(name, src_ap, *toks):
            if name not in dbg_t:
                return

            def g_d(g):
                for tk in toks:
                    W(g, tk)
                g.dma_start(dbg_t[name][:, :], src_ap).then_inc(dDBG.h, 16)
                dDBG.n += 16
            block.gpsimd(g_d)

        perm_sb = SBT("perm_sb", [128, 128], BF16)
        ident_sb = SBT("ident_sb", [128, 128], F32)
        csil_f = SBT("csil_f", [128, NE], F32)
        csil = SBT("csil", [128, NE], F32)
        wg_sb = SBT("wg_sb", [128, NE * NE], BF16)
        b2_sb = SBT("b2_sb", [NE, D], BF16)
        b1c_sb = SBT("b1c_sb", [128, NE * 32], F32)
        bg_bc = SBT("bg_bc", [128, NE], F32)
        stat = SBT("stat", [128, 24], F32)
        st8 = SBT("st8", [128, 48], F32)
        smx = SBT("smx", [128, 64], F32)
        epsc = SBT("epsc", [128, 1], F32)

        def g_eps(g):
            sG.inc(g.memset(epsc[:, :], LN_EPS))
        block.gpsimd(g_eps)

        def g_loads(g):
            for dst, src in [
                (perm_sb[:, :], perm64_in[:, :]),
                (ident_sb[:, :], ident_in[:, :]),
                (csil_f[:, :], c_col[:, :]),
                (wg_sb[:, :], bass.AP(w_g, 0, [[NE, 128], [128 * NE, NE], [1, NE]])),
                (b2_sb[:, :], b2_in[:, :]),
                (b1c_sb[:, :], b1_cols[:, :]),
                (bg_bc[:, :], bg_bc_in[:, :]),
            ]:
                g.dma_start(dst, src).then_inc(dInit.h, 16)
                dInit.n += 16
        block.gpsimd(g_loads)
        _p0 = dInit.tok()

        def sc_silu(sc):
            W(sc, _p0)
            sSC.inc(sc.activation(csil[:, :], csil_f[:, :], AF.Silu))
        block.scalar(sc_silu)
        _csil = sSC.n

        with ExitStack() as es0:
            wmod_ring = es0.enter_context(nc.sbuf_tensor("wmod_ring", [128, 2 * 4096], F32))
            mrows = es0.enter_context(nc.sbuf_tensor("mrows", [1, 6 * D], F32))
            ps_mod = es0.enter_context(nc.psum_tensor("ps_mod", [1, 512], F32))
            mc_dma, mc_mm, mc_ev = [], [], []
            for mc in range(12):
                def g_one(g, mc=mc):
                    if mc >= 2:
                        g.wait_ge(sPE.h, mc_mm[mc - 2])
                    for dc in range(8):
                        src = bass.AP(w_mod, (dc * 128) * 6 * D + mc * 512,
                                      [[6 * D, 128], [1, 512]])
                        g.dma_start(
                            wmod_ring[:, (mc % 2) * 4096 + dc * 512:(mc % 2) * 4096 + dc * 512 + 512],
                            src).then_inc(dWM[mc % 2].h, 16)
                        dWM[mc % 2].n += 16
                    mc_dma.append(dWM[mc % 2].tok())
                block.gpsimd(g_one)

                def pe_one(t, mc=mc):
                    W(t, mc_dma[mc])
                    if mc == 0:
                        t.wait_ge(sSC.h, _csil)
                    if mc >= 1:
                        t.wait_ge(sV.h, mc_ev[mc - 1])
                    ins = None
                    for dc in range(8):
                        base = (mc % 2) * 4096 + dc * 512
                        ins = t.matmul(ps_mod[:, :], csil[:, dc:dc + 1],
                                       wmod_ring[:, base:base + 512],
                                       start=(dc == 0), stop=(dc == 7))
                    sPE.inc(ins)
                    mc_mm.append(sPE.n)
                block.tensor(pe_one)

                def v_one(v, mc=mc):
                    v.wait_ge(sPE.h, mc_mm[mc])
                    sV.inc(v.tensor_copy(mrows[:, mc * 512:mc * 512 + 512], ps_mod[:, :]))
                    mc_ev.append(sV.n)
                block.vector(v_one)

            _rows_v = sV.n

            def sy_rows(sy):
                sy.wait_ge(sV.h, _rows_v)
                for r in range(6):
                    sy.dma_start(rows_hbm[r:r + 1, :],
                                 mrows[:, r * D:(r + 1) * D]).then_inc(dRows.h, 16)
                dRows.n += 96
            block.sync(sy_rows)
            _rows_hbm = dRows.tok()

            dbg_dump("mrows", mrows[:, :], (sV.h, _rows_v))

        def bc_read(sy, dst_ap, row):
            sy.wait_ge(dBC.h, dBC.n)
            sy.dma_start(dst_ap, bass.AP(rows_hbm, row * D, [[0, 128], [1, D]])).then_inc(dBC.h, 16)
            dBC.n += 16

        NHALF = 2 if T > 1024 else 1
        for HALF in range(NHALF):
            TH = T // NHALF
            NT = TH // 128
            NQ = TH // QT
            NC2 = TH // 512
            OFF = HALF * TH
            _sfx = "_h%d" % HALF
            with ExitStack() as esA:
                _a = lambda *x: esA.enter_context(nc.sbuf_tensor(x[0] + _sfx, *x[1:]))
                gam_bc = _a("gam_bc", [128, D], F32)
                bet_bc = _a("bet_bc", [128, D], F32)
                A1 = _a("A1", [128, D], F32)
                SA = _a("SA", [128, D], F32)
                GA = _a("GA", [128, D], F32)
                GABP = _a("GABP", [128, D], F32)
                h1_fm = _a("h1_fm", [128, 8 * TH], BF16)
                xt_ring = _a("xt_ring", [128, 2 * D], F32)
                h1t_ring = _a("h1t_ring", [128, 2 * D], BF16)
                sq_scr = _a("sq_scr", [128, D], F32)
                xhat_ring = _a("xhat_ring", [128, D], F32)
                t1_scr = _a("t1_scr", [128, D], F32)
                barrier_all()

                def sy_bc1(sy):
                    W(sy, _rows_hbm)
                    bc_read(sy, SA[:, :], 0)
                    bc_read(sy, A1[:, :], 1)
                    bc_read(sy, GA[:, :], 2)
                block.sync(sy_bc1)
                _bc1 = dBC.tok()

                def g_bc2(g):
                    g.dma_start(gam_bc[:, :], gam_bc_in[:, :]).then_inc(dInit.h, 16)
                    g.dma_start(bet_bc[:, :], bet_bc_in[:, :]).then_inc(dInit.h, 16)
                    g.dma_start(GABP[:, :], bproj_bc_in[:, :]).then_inc(dInit.h, 16)
                    dInit.n += 48
                block.gpsimd(g_bc2)
                _bc2 = dInit.tok()

                def v_modt(v):
                    W(v, _bc1)
                    W(v, _bc2)
                    sV.inc(v.tensor_scalar(A1[:, :], A1[:, :], 1.0, None, ALU.add))
                    sV.inc(v.tensor_scalar(t1_scr[:, 0:D], GA[:, :], 0.0, None, ALU.is_ge))
                    v.wait_ge(sV.h, sV.n)
                    sV.inc(v.tensor_scalar(t1_scr[:, 0:D], t1_scr[:, 0:D], 2.0, -1.0, ALU.mult, ALU.add))
                    v.wait_ge(sV.h, sV.n)
                    sV.inc(v.tensor_mul(GA[:, :], GA[:, :], t1_scr[:, 0:D]))
                    v.wait_ge(sV.h, sV.n)
                    sV.inc(v.tensor_scalar(GA[:, :], GA[:, :], 0.25, None, ALU.max))
                    v.wait_ge(sV.h, sV.n)
                    sV.inc(v.tensor_mul(GA[:, :], GA[:, :], t1_scr[:, 0:D]))
                    v.wait_ge(sV.h, sV.n)
                    sV.inc(v.tensor_mul(GABP[:, :], GA[:, :], GABP[:, :]))
                block.vector(v_modt)
                _modt = sV.n
                dbg_dump("a1", A1[:, :], (sV.h, _modt))

                xload = [None] * NT
                h1T = [None] * NT
                h1t_done = [0] * NT
                xfree = [0] * NT
                h1v_done = [0] * NT

                for tt in range(NT if main_lvl >= 2 else 0):
                    sl = (tt % 2) * D
                    cb = (tt % 2) * 12

                    def sy_x(sy, tt=tt, sl=sl):
                        if tt >= 2:
                            sy.wait_ge(sV.h, xfree[tt - 2])
                        sy.dma_start(xt_ring[:, sl:sl + D],
                                     x_in[OFF + tt * 128:OFF + tt * 128 + 128, :]).then_inc(dX[tt % 2].h, 16)
                        dX[tt % 2].n += 16
                        xload[tt] = dX[tt % 2].tok()
                    block.sync(sy_x)

                    def sc_sq(sc, tt=tt, sl=sl, cb=cb):
                        W(sc, xload[tt])
                        if tt >= 2:
                            sc.wait_ge(sV.h, h1v_done[tt - 2])
                        sSC.inc(sc.activation(sq_scr[:, 0:D], xt_ring[:, sl:sl + D],
                                              AF.Square, accum_out=stat[:, cb + 1:cb + 2]))
                    block.scalar(sc_sq)
                    _sq = sSC.n

                    def v_st(v, tt=tt, sl=sl, cb=cb, _sq=_sq):
                        if tt >= 2:
                            v.wait_ge(sV.h, h1v_done[tt - 2])
                        W(v, xload[tt])
                        sV.inc(v.reduce_sum(stat[:, cb:cb + 1], xt_ring[:, sl:sl + D], axis=AX.X))
                        v.wait_ge(sSC.h, _sq)
                        v.wait_ge(sV.h, sV.n)
                        sV.inc(v.tensor_scalar(stat[:, cb + 2:cb + 3], stat[:, cb:cb + 1], 1.0 / D, None, ALU.mult))
                        sV.inc(v.tensor_scalar(stat[:, cb + 3:cb + 4], stat[:, cb + 1:cb + 2], 1.0 / D, None, ALU.mult))
                        v.wait_ge(sV.h, sV.n)
                        sV.inc(v.tensor_tensor(stat[:, cb + 4:cb + 5], stat[:, cb + 2:cb + 3],
                                               stat[:, cb + 2:cb + 3], ALU.mult))
                        v.wait_ge(sV.h, sV.n)
                        sV.inc(v.tensor_sub(stat[:, cb + 5:cb + 6], stat[:, cb + 3:cb + 4],
                                            stat[:, cb + 4:cb + 5]))
                    block.vector(v_st)
                    _var = sV.n

                    def sc_rt(sc, cb=cb, _var=_var):
                        sc.wait_ge(sV.h, _var)
                        sc.wait_ge(sG.h, 1)
                        sSC.inc(sc.activation(stat[:, cb + 6:cb + 7], stat[:, cb + 5:cb + 6],
                                              AF.Sqrt, bias=epsc[:, 0:1]))
                    block.scalar(sc_rt)
                    _rt = sSC.n

                    def v_rs(v, cb=cb, _rt=_rt):
                        v.wait_ge(sSC.h, _rt)
                        sV.inc(v.reciprocal(stat[:, cb + 7:cb + 8], stat[:, cb + 6:cb + 7]))
                    block.vector(v_rs)
                    _nm = sV.n

                    def v_xh(v, tt=tt, sl=sl, cb=cb, _nm=_nm):
                        v.wait_ge(sV.h, _nm)
                        if tt >= 2:
                            v.wait_ge(sV.h, h1v_done[tt - 2])
                        sV.inc(v.tensor_scalar(xhat_ring[:, 0:D], xt_ring[:, sl:sl + D],
                                               stat[:, cb + 2:cb + 3], stat[:, cb + 7:cb + 8],
                                               ALU.subtract, ALU.mult))
                        xfree[tt] = sV.n
                    block.vector(v_xh)
                    _xh = sV.n

                    def v_h1(v, tt=tt, sl=sl, _xh=_xh):
                        v.wait_ge(sV.h, _xh)
                        v.wait_ge(sV.h, _modt)
                        if tt >= 2:
                            W(v, h1T[tt - 2])
                        sV.inc(v.tensor_mul(t1_scr[:, 0:D], xhat_ring[:, 0:D], A1[:, :]))
                        v.wait_ge(sV.h, sV.n)
                        sV.inc(v.tensor_add(h1t_ring[:, sl:sl + D], t1_scr[:, 0:D], SA[:, :]))
                        h1t_done[tt] = sV.n
                        h1v_done[tt] = sV.n
                    block.vector(v_h1)

                    dsem = dTs if tt % 2 == 0 else dTa

                    def do_T(eng, tt=tt, sl=sl, dsem=dsem):
                        eng.wait_ge(sV.h, h1t_done[tt])
                        for fb in range(8):
                            eng.dma_start_transpose(
                                h1_fm[:, fb * TH + tt * 128: fb * TH + tt * 128 + 128],
                                h1t_ring[:, sl + fb * 128: sl + fb * 128 + 128]).then_inc(dsem.h, 16)
                        dsem.n += 128
                        h1T[tt] = dsem.tok()
                    block.sync(do_T)

                dbg_dump("h1fm", h1_fm[:, :], dTs.tok(), dTa.tok())

                with ExitStack() as es2:
                    _b = lambda *x: es2.enter_context(nc.sbuf_tensor(x[0] + _sfx, *x[1:]))
                    wqkv_ring = _b("wqkv_ring", [128, 2 * 4096], BF16)
                    wproj_ring = _b("wproj_ring", [128, 2 * 4096], BF16)
                    q_tm = _b("q_tm", [128, 4 * D], BF16)
                    k_tm = _b("k_tm", [128, 4 * D], BF16)
                    v_tm = _b("v_tm", [128, 4 * D], BF16)
                    v_dup = _b("v_dup", [128, 4 * D], BF16)
                    q_fm = _b("q_fm", [128, 8 * QT], BF16)
                    k_fm = _b("k_fm", [128, 8 * QT], BF16)
                    p_s = _b("p_s", [128, 2 * 1024], BF16)
                    p_fm = _b("p_fm", [128, 8 * QT], BF16)
                    attn_fm = _b("attn_fm", [128, 8 * QT], BF16)
                    qn_scr = _b("qn_scr", [128, 512], F32)
                    qsq_scr = _b("qsq_scr", [128, 512], F32)
                    pexp_scr = _b("pexp_scr", [128, D], F32)
                    x2stage = _b("x2stage", [128, 2 * D], F32)
                    ps_qkv = es2.enter_context(nc.psum_tensor("ps_qkv" + _sfx, [128, 2 * 512], F32))
                    ps_a = es2.enter_context(nc.psum_tensor("ps_a" + _sfx, [128, 2 * 512], F32))
                    ps_S = es2.enter_context(nc.psum_tensor("ps_S" + _sfx, [128, 2 * 1024], F32))

                    x2_spilled = [None] * NT
                    x2v = [0] * NT
                    psq = Ring(2)
                    psa = Ring(2)
                    psS = Ring(2)
                    wq_ring = Ring(2)
                    wp_ring = Ring(2)
                    xt2 = Ring(2)
                    x2s_ring = Ring(2)
                    xt2.hist = [(sV.h, xfree[NT - 2]), (sV.h, xfree[NT - 1])]

                    last_qn = sV.n
                    pexp_done = sSC.n
                    pnorm_done = sV.n
                    prev = None

                    for q in range(NQ if main_lvl >= 3 else 0):
                        wq_dma = [None] * 6
                        for c in range(6):
                            def g_wq(g, c=c):
                                wq_ring.guard(g)
                                for dc in range(8):
                                    src = bass.AP(w_qkv, (dc * 128) * 3 * D + c * 512,
                                                  [[3 * D, 128], [1, 512]])
                                    g.dma_start(
                                        wqkv_ring[:, (c % 2) * 4096 + dc * 512:(c % 2) * 4096 + dc * 512 + 512],
                                        src).then_inc(dWQ[c % 2].h, 16)
                                    dWQ[c % 2].n += 16
                                wq_dma[c] = dWQ[c % 2].tok()
                            block.gpsimd(g_wq)

                            for tt in range(4):
                                gt = 4 * q + tt
                                pslot = ((c * 4 + tt) % 2) * 512

                                def pe_qkv(t, c=c, tt=tt, gt=gt, pslot=pslot):
                                    W(t, wq_dma[c])
                                    W(t, h1T[gt])
                                    psq.guard(t)
                                    ins = None
                                    for dc in range(8):
                                        ins = t.matmul(
                                            ps_qkv[:, pslot:pslot + 512],
                                            h1_fm[:, dc * TH + gt * 128: dc * TH + gt * 128 + 128],
                                            wqkv_ring[:, (c % 2) * 4096 + dc * 512:(c % 2) * 4096 + dc * 512 + 512],
                                            start=(dc == 0), stop=(dc == 7))
                                    sPE.inc(ins)
                                block.tensor(pe_qkv)
                                _mm = sPE.n
                                if tt == 3:
                                    wq_ring.done((sPE.h, _mm))

                                if c < 4:
                                    dst = q_tm if c < 2 else k_tm
                                    half = (c % 2) * 512

                                    def sc_qsq(sc, pslot=pslot, _mm=_mm, lq=last_qn):
                                        sc.wait_ge(sPE.h, _mm)
                                        sc.wait_ge(sV.h, lq)
                                        sSC.inc(sc.activation(qsq_scr[:, :],
                                                              ps_qkv[:, pslot:pslot + 512], AF.Square))
                                    block.scalar(sc_qsq)
                                    _qsq = sSC.n

                                    def v_qst(v, pslot=pslot, _mm=_mm, lq=last_qn, _qsq=_qsq):
                                        v.wait_ge(sPE.h, _mm)
                                        v.wait_ge(sV.h, lq)
                                        if prev is not None:
                                            W(v, prev["qkT"][0])
                                            W(v, prev["qkT"][1])
                                            W(v, prev["dbg"])
                                        sV.inc(v.reduce_sum(
                                            st8[:, 0:8],
                                            ps_qkv[:, pslot:pslot + 512].rearrange("p (h k) -> p h k", k=64),
                                            axis=AX.X))
                                        v.wait_ge(sSC.h, _qsq)
                                        sV.inc(v.reduce_sum(
                                            st8[:, 8:16],
                                            qsq_scr[:, :].rearrange("p (h k) -> p h k", k=64),
                                            axis=AX.X))
                                        v.wait_ge(sV.h, sV.n)
                                        sV.inc(v.tensor_scalar(st8[:, 16:24], st8[:, 0:8], 1.0 / 64, None, ALU.mult))
                                        sV.inc(v.tensor_scalar(st8[:, 24:32], st8[:, 8:16], 1.0 / 64, None, ALU.mult))
                                        v.wait_ge(sV.h, sV.n)
                                        sV.inc(v.tensor_tensor(st8[:, 32:40], st8[:, 16:24], st8[:, 16:24], ALU.mult))
                                        v.wait_ge(sV.h, sV.n)
                                        sV.inc(v.tensor_sub(st8[:, 40:48], st8[:, 24:32], st8[:, 32:40]))
                                    block.vector(v_qst)
                                    _qv = sV.n

                                    def sc_qrt(sc, _qv=_qv):
                                        sc.wait_ge(sV.h, _qv)
                                        sc.wait_ge(sG.h, 1)
                                        sSC.inc(sc.activation(st8[:, 8:16], st8[:, 40:48], AF.Sqrt,
                                                              bias=epsc[:, 0:1]))
                                    block.scalar(sc_qrt)
                                    _qrt = sSC.n

                                    def v_qn(v, tt=tt, pslot=pslot, dst=dst, half=half, _qrt=_qrt):
                                        v.wait_ge(sSC.h, _qrt)
                                        sV.inc(v.reciprocal(st8[:, 0:8], st8[:, 8:16]))
                                        v.wait_ge(sV.h, sV.n)
                                        for h in range(8):
                                            sV.inc(v.tensor_scalar(
                                                qn_scr[:, h * 64:h * 64 + 64],
                                                ps_qkv[:, pslot + h * 64:pslot + h * 64 + 64],
                                                st8[:, 16 + h:17 + h], st8[:, 0 + h:1 + h],
                                                ALU.subtract, ALU.mult))
                                        psq.done((sV.h, sV.n))
                                        v.wait_ge(sV.h, sV.n)
                                        sV.inc(v.tensor_mul(qn_scr[:, :], qn_scr[:, :],
                                                            gam_bc[:, half:half + 512]))
                                        v.wait_ge(sV.h, sV.n)
                                        sV.inc(v.tensor_add(dst[:, tt * D + half: tt * D + half + 512],
                                                            qn_scr[:, :], bet_bc[:, half:half + 512]))
                                    block.vector(v_qn)
                                    last_qn = sV.n
                                else:
                                    def v_vev(v, c=c, tt=tt, pslot=pslot, _mm=_mm):
                                        v.wait_ge(sPE.h, _mm)
                                        if prev is not None:
                                            v.wait_ge(sPE.h, prev["av_mm"])
                                            W(v, prev["dbg"])
                                        sV.inc(v.tensor_copy(
                                            v_tm[:, tt * D + (c - 4) * 512: tt * D + (c - 4) * 512 + 512],
                                            ps_qkv[:, pslot:pslot + 512]))
                                        psq.done((sV.h, sV.n))
                                    block.vector(v_vev)

                        _qkv_done = sV.n

                        for i in range(8 if p2_lvl >= 1 else 0):
                            tt, c2 = i // 2, i % 2

                            def pe_vd(t, tt=tt, c2=c2, _qd=_qkv_done):
                                t.wait_ge(sV.h, _qd)
                                psa.guard(t)
                                sPE.inc(t.matmul(
                                    ps_a[:, ((tt * 2 + c2) % 2) * 512:((tt * 2 + c2) % 2) * 512 + 512],
                                    perm_sb[:, :],
                                    v_tm[:, tt * D + c2 * 512: tt * D + c2 * 512 + 512],
                                    start=True, stop=True))
                            block.tensor(pe_vd)
                            _vd = sPE.n

                            def v_vd(v, tt=tt, c2=c2, _vd=_vd):
                                v.wait_ge(sPE.h, _vd)
                                if prev is not None:
                                    v.wait_ge(sPE.h, prev["av_mm"])
                                    W(v, prev["dbg"])
                                sV.inc(v.tensor_copy(
                                    v_dup[:, tt * D + c2 * 512: tt * D + c2 * 512 + 512],
                                    ps_a[:, ((tt * 2 + c2) % 2) * 512:((tt * 2 + c2) % 2) * 512 + 512]))
                                psa.done((sV.h, sV.n))
                            block.vector(v_vd)
                        _vdup = sV.n

                        qkT = [None] * 4
                        for tt in range(4 if p2_lvl >= 2 else 0):
                            dsem = dQs if tt % 2 == 0 else dQa

                            def do_qkT(eng, tt=tt, dsem=dsem, _qd=_qkv_done):
                                eng.wait_ge(sV.h, _qd)
                                if prev is not None:
                                    eng.wait_ge(sPE.h, prev["S_mm"])
                                for fb in range(8):
                                    eng.dma_start_transpose(
                                        q_fm[:, fb * QT + tt * 128: fb * QT + tt * 128 + 128],
                                        q_tm[:, tt * D + fb * 128: tt * D + fb * 128 + 128]).then_inc(dsem.h, 16)
                                    eng.dma_start_transpose(
                                        k_fm[:, fb * QT + tt * 128: fb * QT + tt * 128 + 128],
                                        k_tm[:, tt * D + fb * 128: tt * D + fb * 128 + 128]).then_inc(dsem.h, 16)
                                dsem.n += 256
                                qkT[tt] = dsem.tok()
                            block.sync(do_qkT)

                        if q == 0:
                            dbg_dump("qtm", q_tm[:, :], (sV.h, _qkv_done))
                            dbg_dump("ktm", k_tm[:, :], (sV.h, _qkv_done))
                            dbg_dump("vtm", v_tm[:, :], (sV.h, _vdup))
                            dbg_dump("qfm", q_fm[:, :], qkT[3], qkT[2])

                        S_mm = [0] * 4
                        pT = [None] * 4
                        for tt in range(4 if p2_lvl >= 3 else 0):
                            Sslot = (tt % 2) * 1024

                            def pe_S(t, tt=tt, Sslot=Sslot):
                                W(t, qkT[2])
                                W(t, qkT[3])
                                psS.guard(t)
                                ins = None
                                for h in range(NH):
                                    hb = (h % 2) * 64
                                    for wi in range(2):
                                        col = (h // 2) * QT + tt * 128 + wi * 64
                                        blk = wi * 8 + h // 2
                                        ins = t.matmul(
                                            ps_S[hb:hb + 64, Sslot + blk * 64:Sslot + blk * 64 + 64],
                                            q_fm[hb:hb + 64, col:col + 64],
                                            k_fm[hb:hb + 64, col:col + 64],
                                            start=True, stop=True,
                                            tile_position=(hb, hb))
                                sPE.inc(ins)
                                S_mm[tt] = sPE.n
                            block.tensor(pe_S)

                            if s3_lvl < 2:
                                continue

                            def v_negm(v, tt=tt, Sslot=Sslot, pd=pexp_done):
                                v.wait_ge(sPE.h, S_mm[tt])
                                v.wait_ge(sSC.h, pd)
                                sV.inc(v.reduce_max(
                                    smx[:, 0:16],
                                    ps_S[:, Sslot:Sslot + 1024].rearrange("p (h k) -> p h k", k=64),
                                    axis=AX.X, negate=True))
                                v.wait_ge(sV.h, sV.n)
                                sV.inc(v.tensor_scalar(smx[:, 16:32], smx[:, 0:16], 0.125, None, ALU.mult))
                            block.vector(v_negm)
                            _ng = sV.n

                            if s3_lvl < 3:
                                continue

                            def sc_exp(sc, tt=tt, Sslot=Sslot, _ng=_ng, pn=pnorm_done):
                                sc.wait_ge(sV.h, _ng)
                                sc.wait_ge(sV.h, pn)
                                ins = None
                                for h in range(NH):
                                    ins = sc.activation(pexp_scr[:, h * 64:h * 64 + 64],
                                                        ps_S[:, Sslot + h * 64:Sslot + h * 64 + 64],
                                                        AF.Exp, bias=smx[:, 16 + h:17 + h], scale=0.125)
                                sSC.inc(ins)
                            block.scalar(sc_exp)
                            pexp_done = sSC.n
                            psS.done((sSC.h, pexp_done))

                            if s3_lvl < 4:
                                continue

                            def v_pn(v, tt=tt, pd=pexp_done):
                                v.wait_ge(sSC.h, pd)
                                if tt >= 2:
                                    W(v, pT[tt - 2])
                                elif prev is not None:
                                    W(v, prev["pT"][0])
                                    W(v, prev["pT"][1])
                                    W(v, prev["dbg"])
                                sV.inc(v.reduce_sum(
                                    smx[:, 32:48],
                                    pexp_scr[:, :].rearrange("p (h k) -> p h k", k=64),
                                    axis=AX.X))
                                v.wait_ge(sV.h, sV.n)
                                sV.inc(v.reciprocal(smx[:, 48:64], smx[:, 32:48]))
                                v.wait_ge(sV.h, sV.n)
                                for b in range(16):
                                    sV.inc(v.tensor_scalar(
                                        p_s[:, (tt % 2) * 1024 + b * 64:(tt % 2) * 1024 + b * 64 + 64],
                                        pexp_scr[:, b * 64:b * 64 + 64],
                                        smx[:, 48 + b:49 + b], None, ALU.mult))
                            block.vector(v_pn)
                            pnorm_done = sV.n
                            _pn = sV.n

                            if s3_lvl < 5:
                                continue
                            dsem = dPs if tt % 2 == 0 else dPa

                            def do_pT(eng, tt=tt, dsem=dsem, _pn=_pn):
                                eng.wait_ge(sV.h, _pn)
                                if prev is not None:
                                    eng.wait_ge(sPE.h, prev["av_mm"])
                                for wi in range(2):
                                    for eps in range(2):
                                        for jj in range(4):
                                            eng.dma_start_transpose(
                                                p_fm[:, (eps * 4 + jj) * QT + tt * 128 + wi * 64:
                                                     (eps * 4 + jj) * QT + tt * 128 + wi * 64 + 64],
                                                p_s[eps * 64:eps * 64 + 64,
                                                    (tt % 2) * 1024 + (wi * 8 + 2 * jj) * 64:
                                                    (tt % 2) * 1024 + (wi * 8 + 2 * jj) * 64 + 128]).then_inc(dsem.h, 16)
                                dsem.n += 256
                                pT[tt] = dsem.tok()
                            block.sync(do_pT)



                        for bp in range(8 if p2_lvl >= 4 else 0):
                            def pe_av(t, bp=bp, _vdup=_vdup):
                                W(t, pT[2])
                                W(t, pT[3])
                                t.wait_ge(sV.h, _vdup)
                                psa.guard(t)
                                eps, jj = bp // 4, bp % 4
                                ins = None
                                for pos in range(2):
                                    h = 4 * jj + eps + 2 * pos
                                    beta = pos * 64
                                    for tt in range(4):
                                        for wi in range(2):
                                            vbuf = v_tm if wi * 64 == beta else v_dup
                                            ins = t.matmul(
                                                ps_a[beta:beta + 64,
                                                     (bp % 2) * 512 + tt * 128 + wi * 64:
                                                     (bp % 2) * 512 + tt * 128 + wi * 64 + 64],
                                                vbuf[beta:beta + 64, tt * D + h * 64: tt * D + h * 64 + 64],
                                                p_fm[beta:beta + 64,
                                                     bp * QT + tt * 128 + wi * 64:
                                                     bp * QT + tt * 128 + wi * 64 + 64],
                                                start=True, stop=True,
                                                tile_position=(beta, beta))
                                sPE.inc(ins)
                            block.tensor(pe_av)
                            _avm = sPE.n

                            def v_av(v, bp=bp, _avm=_avm):
                                v.wait_ge(sPE.h, _avm)
                                if prev is not None:
                                    v.wait_ge(sPE.h, prev["proj_mm"])
                                    W(v, prev["dbg"])
                                sV.inc(v.tensor_copy(attn_fm[:, bp * 512:bp * 512 + 512],
                                                     ps_a[:, (bp % 2) * 512:(bp % 2) * 512 + 512]))
                                psa.done((sV.h, sV.n))
                            block.vector(v_av)
                        _attn = sV.n
                        av_mm_final = sPE.n

                        if q == 0:
                            dbg_dump("attnfm", attn_fm[:, :], (sV.h, _attn))

                        wp_dma = [None] * 2
                        for dg in range(2 if p2_lvl >= 5 else 0):
                            def g_wp(g, dg=dg):
                                wp_ring.guard(g)
                                for bp in range(8):
                                    eps, jj = bp // 4, bp % 4
                                    row0 = (4 * jj + eps) * 64
                                    src = bass.AP(w_proj, row0 * D + dg * 512,
                                                  [[128 * D, 2], [D, 64], [1, 512]])
                                    g.dma_start(
                                        wproj_ring[:, (dg % 2) * 4096 + bp * 512:(dg % 2) * 4096 + bp * 512 + 512],
                                        src).then_inc(dWP[dg].h, 16)
                                dWP[dg].n += 128
                                wp_dma[dg] = dWP[dg].tok()
                            block.gpsimd(g_wp)

                        xrl = [None] * 4
                        for tt in range(4 if p2_lvl >= 5 else 0):
                            gt = 4 * q + tt

                            def sy_xr(sy, tt=tt, gt=gt):
                                xt2.guard(sy)
                                sy.dma_start(xt_ring[:, (tt % 2) * D:(tt % 2) * D + D],
                                             x_in[OFF + gt * 128:OFF + gt * 128 + 128, :]).then_inc(dX[tt % 2].h, 16)
                                dX[tt % 2].n += 16
                                xrl[tt] = dX[tt % 2].tok()
                            block.sync(sy_xr)

                            for dg in range(2):
                                pslot = ((tt * 2 + dg) % 2) * 512

                                def pe_pr(t, tt=tt, dg=dg, pslot=pslot, _attn=_attn):
                                    W(t, wp_dma[dg])
                                    t.wait_ge(sV.h, _attn)
                                    psq.guard(t)
                                    ins = None
                                    for dcp in range(8):
                                        ins = t.matmul(
                                            ps_qkv[:, pslot:pslot + 512],
                                            attn_fm[:, dcp * 512 + tt * 128: dcp * 512 + tt * 128 + 128],
                                            wproj_ring[:, (dg % 2) * 4096 + dcp * 512:(dg % 2) * 4096 + dcp * 512 + 512],
                                            start=(dcp == 0), stop=(dcp == 7))
                                    sPE.inc(ins)
                                block.tensor(pe_pr)
                                _prm = sPE.n
                                if tt == 3:
                                    wp_ring.done((sPE.h, _prm))

                                def v_ep(v, tt=tt, dg=dg, pslot=pslot, gt=gt, _prm=_prm):
                                    v.wait_ge(sPE.h, _prm)
                                    W(v, xrl[tt])
                                    if dg == 0:
                                        x2s_ring.guard(v)
                                    sV.inc(v.tensor_mul(t1_scr[:, dg * 512:dg * 512 + 512],
                                                        ps_qkv[:, pslot:pslot + 512],
                                                        GA[:, dg * 512:dg * 512 + 512]))
                                    psq.done((sV.h, sV.n))
                                    v.wait_ge(sV.h, sV.n)
                                    sV.inc(v.tensor_add(
                                        t1_scr[:, dg * 512:dg * 512 + 512],
                                        t1_scr[:, dg * 512:dg * 512 + 512],
                                        xt_ring[:, (tt % 2) * D + dg * 512:(tt % 2) * D + dg * 512 + 512]))
                                    v.wait_ge(sV.h, sV.n)
                                    sV.inc(v.tensor_add(
                                        x2stage[:, (tt % 2) * D + dg * 512:(tt % 2) * D + dg * 512 + 512],
                                        t1_scr[:, dg * 512:dg * 512 + 512],
                                        GABP[:, dg * 512:dg * 512 + 512]))
                                    if dg == 1:
                                        x2v[gt] = sV.n
                                        xt2.done((sV.h, sV.n))
                                block.vector(v_ep)

                            def sy_sp(sy, tt=tt, gt=gt):
                                sy.wait_ge(sV.h, x2v[gt])
                                sy.dma_start(x2_hbm[OFF + gt * 128:OFF + gt * 128 + 128, :],
                                             x2stage[:, (tt % 2) * D:(tt % 2) * D + D]).then_inc(dX2[gt % 2].h, 16)
                                dX2[gt % 2].n += 16
                                x2_spilled[gt] = dX2[gt % 2].tok()
                                x2s_ring.done(x2_spilled[gt])
                            block.sync(sy_sp)

                        prev = {"S_mm": S_mm[3], "av_mm": av_mm_final,
                                "proj_mm": sPE.n, "qkT": [qkT[2], qkT[3]],
                                "pT": [pT[2], pT[3]], "dbg": dDBG.tok()}

            with ExitStack() as es3:
                acc = es3.enter_context(nc.sbuf_tensor("acc" + _sfx, [128, NT * D], F32))
                h2_fm = es3.enter_context(nc.sbuf_tensor("h2_fm" + _sfx, [128, 8 * TH], BF16))
                comb = es3.enter_context(nc.sbuf_tensor("comb" + _sfx, [128, NT * 8], F32))
                c_rT = es3.enter_context(nc.sbuf_tensor("c_rT" + _sfx, [8, TH], BF16))
                lgb = es3.enter_context(nc.sbuf_tensor("lgb" + _sfx, [128, NT * 8], F32))
                ps_g = es3.enter_context(nc.psum_tensor("ps_g" + _sfx, [128, NE], F32))
                ps_cT = es3.enter_context(nc.psum_tensor("ps_cT" + _sfx, [8, 128], F32))

                h2T = [None] * NT
                acc_done = [0] * NT
                barrier_all()

                with ExitStack() as es3a:
                    x2r = es3a.enter_context(nc.sbuf_tensor("x2r" + _sfx, [128, 2 * D], F32))
                    h2t_ring = es3a.enter_context(nc.sbuf_tensor("h2t_ring" + _sfx, [128, 2 * D], BF16))
                    xh2 = es3a.enter_context(nc.sbuf_tensor("xh2" + _sfx, [128, 2 * D], F32))
                    sq2 = es3a.enter_context(nc.sbuf_tensor("sq2" + _sfx, [128, 2 * D], F32))
                    B1t = es3a.enter_context(nc.sbuf_tensor("B1t" + _sfx, [128, D], F32))
                    SBt = es3a.enter_context(nc.sbuf_tensor("SBt" + _sfx, [128, D], F32))

                    if main_lvl < 4:
                        pass

                    def sy_bc3(sy):
                        W(sy, _rows_hbm)
                        bc_read(sy, SBt[:, :], 3)
                        bc_read(sy, B1t[:, :], 4)
                    block.sync(sy_bc3)
                    _bc3 = dBC.tok()

                    def v_b1(v):
                        W(v, _bc3)
                        sV.inc(v.tensor_scalar(B1t[:, :], B1t[:, :], 1.0, None, ALU.add))
                    block.vector(v_b1)
                    _b1d = sV.n

                    x2l = [None] * NT
                    x2free = [0] * NT

                    for tt in range(NT if main_lvl >= 4 else 0):
                        sl = (tt % 2) * D
                        cb = (tt % 2) * 12

                        def sy_x2(sy, tt=tt, sl=sl):
                            if tt >= 2:
                                sy.wait_ge(sV.h, x2free[tt - 2])
                            W(sy, x2_spilled[tt])
                            sy.dma_start(x2r[:, sl:sl + D],
                                         x2_hbm[OFF + tt * 128:OFF + tt * 128 + 128, :]).then_inc(dXR[tt % 2].h, 16)
                            dXR[tt % 2].n += 16
                            x2l[tt] = dXR[tt % 2].tok()
                        block.sync(sy_x2)

                        def sc_sq2(sc, tt=tt, sl=sl, cb=cb):
                            W(sc, x2l[tt])
                            if tt >= 2:
                                sc.wait_ge(sV.h, acc_done[tt - 2])
                            sSC.inc(sc.activation(sq2[:, sl:sl + D], x2r[:, sl:sl + D],
                                                  AF.Square, accum_out=stat[:, cb + 1:cb + 2]))
                        block.scalar(sc_sq2)
                        _s2 = sSC.n

                        def v_st2(v, tt=tt, sl=sl, cb=cb, _s2=_s2):
                            if tt >= 2:
                                v.wait_ge(sV.h, acc_done[tt - 2])
                            W(v, x2l[tt])
                            sV.inc(v.reduce_sum(stat[:, cb:cb + 1], x2r[:, sl:sl + D], axis=AX.X))
                            v.wait_ge(sSC.h, _s2)
                            v.wait_ge(sV.h, sV.n)
                            sV.inc(v.tensor_scalar(stat[:, cb + 2:cb + 3], stat[:, cb:cb + 1], 1.0 / D, None, ALU.mult))
                            sV.inc(v.tensor_scalar(stat[:, cb + 3:cb + 4], stat[:, cb + 1:cb + 2], 1.0 / D, None, ALU.mult))
                            v.wait_ge(sV.h, sV.n)
                            sV.inc(v.tensor_tensor(stat[:, cb + 4:cb + 5], stat[:, cb + 2:cb + 3],
                                                   stat[:, cb + 2:cb + 3], ALU.mult))
                            v.wait_ge(sV.h, sV.n)
                            sV.inc(v.tensor_sub(stat[:, cb + 5:cb + 6], stat[:, cb + 3:cb + 4],
                                                stat[:, cb + 4:cb + 5]))
                        block.vector(v_st2)
                        _v2 = sV.n

                        def sc_rt2(sc, cb=cb, _v2=_v2):
                            sc.wait_ge(sV.h, _v2)
                            sc.wait_ge(sG.h, 1)
                            sSC.inc(sc.activation(stat[:, cb + 6:cb + 7], stat[:, cb + 5:cb + 6],
                                                  AF.Sqrt, bias=epsc[:, 0:1]))
                        block.scalar(sc_rt2)
                        _r2 = sSC.n

                        def v_rs2(v, cb=cb, _r2=_r2):
                            v.wait_ge(sSC.h, _r2)
                            sV.inc(v.reciprocal(stat[:, cb + 7:cb + 8], stat[:, cb + 6:cb + 7]))
                        block.vector(v_rs2)
                        _n2 = sV.n

                        def v_xh2(v, tt=tt, sl=sl, cb=cb, _n2=_n2):
                            v.wait_ge(sV.h, _n2)
                            if tt >= 2:
                                v.wait_ge(sV.h, acc_done[tt - 2])
                            sV.inc(v.tensor_scalar(xh2[:, sl:sl + D], x2r[:, sl:sl + D],
                                                   stat[:, cb + 2:cb + 3], stat[:, cb + 7:cb + 8],
                                                   ALU.subtract, ALU.mult))
                            x2free[tt] = sV.n
                        block.vector(v_xh2)
                        _x2h = sV.n

                        def v_acc(v, tt=tt, sl=sl, _x2h=_x2h):
                            v.wait_ge(sV.h, _x2h)
                            v.wait_ge(sV.h, _b1d)
                            if tt >= 2:
                                W(v, h2T[tt - 2])
                            sV.inc(v.tensor_mul(sq2[:, sl:sl + D], xh2[:, sl:sl + D], B1t[:, :]))
                            v.wait_ge(sV.h, sV.n)
                            sV.inc(v.tensor_add(acc[:, tt * D:tt * D + D], sq2[:, sl:sl + D], SBt[:, :]))
                            v.wait_ge(sV.h, sV.n)
                            sV.inc(v.tensor_copy(h2t_ring[:, sl:sl + D], acc[:, tt * D:tt * D + D]))
                            acc_done[tt] = sV.n
                        block.vector(v_acc)

                        dsem = dTs if tt % 2 == 0 else dTa

                        def do_T2(eng, tt=tt, sl=sl, dsem=dsem):
                            eng.wait_ge(sV.h, acc_done[tt])
                            for fb in range(8):
                                eng.dma_start_transpose(
                                    h2_fm[:, fb * TH + tt * 128: fb * TH + tt * 128 + 128],
                                    h2t_ring[:, sl + fb * 128: sl + fb * 128 + 128]).then_inc(dsem.h, 16)
                            dsem.n += 128
                            h2T[tt] = dsem.tok()
                        block.sync(do_T2)

                    dbg_dump("acc0", acc[:, :], (sV.h, sV.n))
                    dbg_dump("h2fm", h2_fm[:, :], dTs.tok(), dTa.tok())

                    lg_ev = [0] * NT
                    cT_ev = [0] * NT
                    for tt in range(NT if main_lvl >= 4 else 0):
                        def pe_g(t, tt=tt):
                            W(t, h2T[tt])
                            if tt >= 1:
                                t.wait_ge(sV.h, lg_ev[tt - 1])
                            ins = None
                            for dc in range(8):
                                ins = t.matmul(ps_g[:, :],
                                               h2_fm[:, dc * TH + tt * 128: dc * TH + tt * 128 + 128],
                                               wg_sb[:, dc * NE:(dc + 1) * NE],
                                               start=(dc == 0), stop=(dc == 7))
                            sPE.inc(ins)
                        block.tensor(pe_g)
                        _gm = sPE.n

                        def v_t2(v, tt=tt, _gm=_gm):
                            v.wait_ge(sPE.h, _gm)
                            if tt == 0:
                                v.wait_ge(sSC.h, pexp_done)
                            L = lgb[:, tt * 8:tt * 8 + 8]
                            sV.inc(v.tensor_add(L, ps_g[:, :], bg_bc[:, :]))
                            lg_ev[tt] = sV.n
                            v.wait_ge(sV.h, sV.n)
                            sV.inc(v.reduce_max(st8[:, 0:1], L, axis=AX.X))
                            v.wait_ge(sV.h, sV.n)
                            sV.inc(v.tensor_scalar(st8[:, 8:16], L, st8[:, 0:1], None, ALU.is_equal))
                            v.wait_ge(sV.h, sV.n)
                            sV.inc(v.scalar_tensor_tensor(st8[:, 16:24], st8[:, 8:16], -1e30, L,
                                                          ALU.mult, ALU.add))
                            v.wait_ge(sV.h, sV.n)
                            sV.inc(v.reduce_max(st8[:, 1:2], st8[:, 16:24], axis=AX.X))
                            v.wait_ge(sV.h, sV.n)
                            sV.inc(v.tensor_sub(st8[:, 2:3], st8[:, 1:2], st8[:, 0:1]))
                        block.vector(v_t2)
                        _t2 = sV.n

                        def sc_sig(sc, _t2=_t2):
                            sc.wait_ge(sV.h, _t2)
                            sSC.inc(sc.activation(st8[:, 3:4], st8[:, 2:3], AF.Sigmoid))
                        block.scalar(sc_sig)
                        _sgm = sSC.n

                        def v_cmb(v, tt=tt, _sgm=_sgm):
                            v.wait_ge(sSC.h, _sgm)
                            sV.inc(v.tensor_scalar(st8[:, 4:5], st8[:, 3:4], -1.0, 1.0, ALU.mult, ALU.add))
                            sV.inc(v.tensor_scalar(st8[:, 24:32], st8[:, 16:24], st8[:, 1:2], None, ALU.is_equal))
                            v.wait_ge(sV.h, sV.n)
                            sV.inc(v.tensor_scalar(comb[:, tt * 8:tt * 8 + 8], st8[:, 8:16],
                                                   st8[:, 4:5], None, ALU.mult))
                            v.wait_ge(sV.h, sV.n)
                            sV.inc(v.scalar_tensor_tensor(comb[:, tt * 8:tt * 8 + 8], st8[:, 24:32],
                                                          st8[:, 3:4], comb[:, tt * 8:tt * 8 + 8],
                                                          ALU.mult, ALU.add))
                        block.vector(v_cmb)
                        _cb2 = sV.n

                        def pe_cT(t, tt=tt, _cb2=_cb2):
                            t.wait_ge(sV.h, _cb2)
                            if tt >= 1:
                                t.wait_ge(sV.h, cT_ev[tt - 1])
                            sPE.inc(t.transpose(ps_cT[:, :], comb[:, tt * 8:tt * 8 + 8], ident_sb[:, :]))
                        block.tensor(pe_cT)
                        _ctm = sPE.n

                        def v_cT(v, tt=tt, _ctm=_ctm):
                            v.wait_ge(sPE.h, _ctm)
                            sV.inc(v.tensor_copy(c_rT[:, tt * 128:tt * 128 + 128], ps_cT[0:8, :]))
                            cT_ev[tt] = sV.n
                        block.vector(v_cT)

                    dbg_dump("comb", comb[:, :], (sV.h, sV.n))
                    dbg_dump("logits", lgb[:, :], (sV.h, sV.n))

                _comb_done = sV.n
                with ExitStack() as es4:
                    wslot = es4.enter_context(nc.sbuf_tensor("wslot" + _sfx, [128, 3 * 8192], BF16))
                    g_bf = es4.enter_context(nc.sbuf_tensor("g_bf" + _sfx, [128, 8 * TH], BF16))
                    ps_z = es4.enter_context(nc.psum_tensor("ps_z" + _sfx, [128, 2 * 512], F32))
                    ps_y = es4.enter_context(nc.psum_tensor("ps_y" + _sfx, [128, 2 * 512], F32))

                    n_chunks = NE * 4 * 2
                    wdma = [None] * n_chunks
                    wuse = []
                    next_load = [0]

                    def emit_load(ci):
                        e, hq, wkind = ci // 8, (ci // 2) % 4, ci % 2

                        def g_w(g, ci=ci, e=e, hq=hq, wkind=wkind):
                            if ci >= 3:
                                g.wait_ge(sPE.h, wuse[ci - 3])
                            for dc in range(8):
                                if wkind == 0:
                                    src = bass.AP(w1, (dc * 128) * HF + e * D * HF + hq * 1024,
                                                  [[HF, 128], [1, 1024]])
                                else:
                                    src = bass.AP(w2, (dc * 128) * D + e * HF * D + hq * 1024 * D,
                                                  [[D, 128], [1, 1024]])
                                g.dma_start(
                                    wslot[:, (ci % 3) * 8192 + dc * 1024:(ci % 3) * 8192 + dc * 1024 + 1024],
                                    src).then_inc(dW[ci % 3].h, 16)
                                dW[ci % 3].n += 16
                            wdma[ci] = dW[ci % 3].tok()
                        block.gpsimd(g_w)

                    def pump_loads():
                        while next_load[0] < n_chunks and (next_load[0] < 3 or next_load[0] - 3 < len(wuse)):
                            emit_load(next_load[0])
                            next_load[0] += 1

                    barrier_all()
                    if p4_e > 0:
                        pump_loads()

                    psz = Ring(2)
                    psy = Ring(2)
                    yacc = {}
                    prev_y_mm = 0

                    for tt in range(NT if p4_e > 0 else 0):
                        for dg in range(2):
                            yp = ((tt * 2 + dg) % 2) * 512

                            def pe_b2(t, tt=tt, dg=dg, yp=yp):
                                t.wait_ge(sV.h, _comb_done)
                                psy.guard(t)
                                sPE.inc(t.matmul(ps_y[:, yp:yp + 512],
                                                 c_rT[0:8, tt * 128:tt * 128 + 128],
                                                 b2_sb[0:8, dg * 512:dg * 512 + 512],
                                                 start=True, stop=True))
                            block.tensor(pe_b2)
                            _b2m = sPE.n

                            def v_b2(v, tt=tt, dg=dg, yp=yp, _b2m=_b2m):
                                v.wait_ge(sPE.h, _b2m)
                                v.wait_ge(sV.h, _comb_done)
                                sV.inc(v.tensor_add(
                                    acc[:, tt * D + dg * 512: tt * D + dg * 512 + 512],
                                    acc[:, tt * D + dg * 512: tt * D + dg * 512 + 512],
                                    ps_y[:, yp:yp + 512]))
                                yacc[(tt, dg)] = sV.n
                                psy.done((sV.h, sV.n))
                            block.vector(v_b2)

                    for e in range(min(NE, p4_e) if p4_e > 0 else 0):
                        for hq in range(4):
                            ci = (e * 4 + hq) * 2
                            s1 = (ci % 3) * 8192
                            s2 = ((ci + 1) % 3) * 8192

                            for hs in range(8):
                                for tc in range(NC2):
                                    zp = ((hs * NC2 + tc) % 2) * 512

                                    def pe_z(t, e=e, hq=hq, hs=hs, tc=tc, zp=zp, s1=s1, ci=ci):
                                        W(t, wdma[ci])
                                        if (e, hq, hs, tc) == (0, 0, 0, 0):
                                            t.wait_ge(sV.h, _comb_done)
                                        psz.guard(t)
                                        ins = None
                                        for dc in range(8):
                                            ins = t.matmul(
                                                ps_z[:, zp:zp + 512],
                                                wslot[:, s1 + dc * 1024 + hs * 128: s1 + dc * 1024 + hs * 128 + 128],
                                                h2_fm[:, dc * TH + tc * 512: dc * TH + tc * 512 + 512],
                                                start=(dc == 0), stop=(dc == 7))
                                        sPE.inc(ins)
                                        if hs == 7 and tc == NC2 - 1:
                                            wuse.append(sPE.n)
                                    block.tensor(pe_z)
                                    _zm = sPE.n

                                    def sc_gl(sc, e=e, hq=hq, hs=hs, tc=tc, zp=zp, _zm=_zm,
                                              pym=prev_y_mm, dbg_tok=None):
                                        sc.wait_ge(sPE.h, _zm)
                                        if hs == 0 and tc == 0:
                                            sc.wait_ge(sPE.h, pym)
                                            sc.wait_ge(dDBG.h, dDBG.n)
                                        bcol = e * 32 + hq * 8 + hs
                                        sSC.inc(sc.activation(
                                            g_bf[:, hs * TH + tc * 512: hs * TH + tc * 512 + 512],
                                            ps_z[:, zp:zp + 512], AF.Gelu,
                                            bias=b1c_sb[:, bcol:bcol + 1]))
                                        psz.done((sSC.h, sSC.n))
                                    block.scalar(sc_gl)

                            _gelu_all = sSC.n
                            pump_loads()

                            if (e, hq) == (0, 0):
                                dbg_dump("g0", g_bf[:, :], (sSC.h, _gelu_all))

                            for tt in range(NT):
                                for dg in range(2):
                                    yp = ((tt * 2 + dg) % 2) * 512

                                    def pe_y(t, e=e, hq=hq, tt=tt, dg=dg, yp=yp, s2=s2, ci=ci,
                                             _ga=_gelu_all):
                                        W(t, wdma[ci + 1])
                                        t.wait_ge(sSC.h, _ga)
                                        psy.guard(t)
                                        ins = None
                                        for hs in range(8):
                                            ins = t.matmul(
                                                ps_y[:, yp:yp + 512],
                                                g_bf[:, hs * TH + tt * 128: hs * TH + tt * 128 + 128],
                                                wslot[:, s2 + hs * 1024 + dg * 512: s2 + hs * 1024 + dg * 512 + 512],
                                                start=(hs == 0), stop=(hs == 7))
                                        sPE.inc(ins)
                                        if tt == NT - 1 and dg == 1:
                                            wuse.append(sPE.n)
                                    block.tensor(pe_y)
                                    _ym = sPE.n

                                    def v_ev(v, e=e, tt=tt, dg=dg, yp=yp, _ym=_ym):
                                        v.wait_ge(sPE.h, _ym)
                                        v.wait_ge(sV.h, yacc[(tt, dg)])
                                        sV.inc(v.scalar_tensor_tensor(
                                            acc[:, tt * D + dg * 512: tt * D + dg * 512 + 512],
                                            ps_y[:, yp:yp + 512],
                                            comb[:, tt * 8 + e: tt * 8 + e + 1],
                                            acc[:, tt * D + dg * 512: tt * D + dg * 512 + 512],
                                            ALU.mult, ALU.add))
                                        yacc[(tt, dg)] = sV.n
                                        psy.done((sV.h, sV.n))
                                    block.vector(v_ev)

                            prev_y_mm = sPE.n
                            pump_loads()

                _ffn = sV.n
                dbg_dump("accf", acc[:, :], (sV.h, _ffn))

                with ExitStack() as es5:
                    x2r2 = es5.enter_context(nc.sbuf_tensor("x2r2" + _sfx, [128, 2 * D], F32))
                    xh5 = es5.enter_context(nc.sbuf_tensor("xh5" + _sfx, [128, 2 * D], F32))
                    outst = es5.enter_context(nc.sbuf_tensor("outst" + _sfx, [128, 2 * D], F32))
                    GFB = es5.enter_context(nc.sbuf_tensor("GFB" + _sfx, [128, D], F32))
                    BFB = es5.enter_context(nc.sbuf_tensor("BFB" + _sfx, [128, D], F32))
                    gbt = es5.enter_context(nc.sbuf_tensor("gbt" + _sfx, [128, D], F32))
                    barrier_all()

                    def sy_bc5(sy):
                        W(sy, _rows_hbm)
                        bc_read(sy, gbt[:, :], 5)
                    block.sync(sy_bc5)
                    _bc5 = dBC.tok()

                    def g_bc5(g):
                        g.wait_ge(dInit.h, dInit.n)
                        g.dma_start(GFB[:, :], gf_bc_in[:, :]).then_inc(dInit.h, 16)
                        g.dma_start(BFB[:, :], bf_bc_in[:, :]).then_inc(dInit.h, 16)
                        dInit.n += 32
                    block.gpsimd(g_bc5)
                    _bc5g = dInit.tok()

                    def v_gfb(v):
                        W(v, _bc5)
                        W(v, _bc5g)
                        v.wait_ge(sV.h, _ffn)
                        sV.inc(v.tensor_scalar(xh5[:, 0:D], gbt[:, :], 0.0, None, ALU.is_ge))
                        v.wait_ge(sV.h, sV.n)
                        sV.inc(v.tensor_scalar(xh5[:, 0:D], xh5[:, 0:D], 2.0, -1.0, ALU.mult, ALU.add))
                        v.wait_ge(sV.h, sV.n)
                        sV.inc(v.tensor_mul(gbt[:, :], gbt[:, :], xh5[:, 0:D]))
                        v.wait_ge(sV.h, sV.n)
                        sV.inc(v.tensor_scalar(gbt[:, :], gbt[:, :], 0.25, None, ALU.max))
                        v.wait_ge(sV.h, sV.n)
                        sV.inc(v.tensor_mul(gbt[:, :], gbt[:, :], xh5[:, 0:D]))
                        v.wait_ge(sV.h, sV.n)
                        sV.inc(v.tensor_mul(GFB[:, :], GFB[:, :], gbt[:, :]))
                        sV.inc(v.tensor_mul(BFB[:, :], BFB[:, :], gbt[:, :]))
                    block.vector(v_gfb)
                    _gfb = sV.n

                    out_dma = [None] * NT
                    x2l5 = [None] * NT
                    outv = [0] * NT

                    for tt in range(min(NT, p5_n) if main_lvl >= 6 else 0):
                        sl = (tt % 2) * D
                        cb = (tt % 2) * 12

                        def sy_x5(sy, tt=tt, sl=sl):
                            if tt >= 2:
                                sy.wait_ge(sV.h, outv[tt - 2])
                            sy.dma_start(x2r2[:, sl:sl + D],
                                         x2_hbm[OFF + tt * 128:OFF + tt * 128 + 128, :]).then_inc(dXR[tt % 2].h, 16)
                            dXR[tt % 2].n += 16
                            x2l5[tt] = dXR[tt % 2].tok()
                        block.sync(sy_x5)

                        def sc_sq5(sc, tt=tt, sl=sl, cb=cb):
                            sc.wait_ge(sV.h, _gfb)
                            if tt >= 2:
                                sc.wait_ge(sV.h, outv[tt - 2])
                            sSC.inc(sc.activation(xh5[:, sl:sl + D], acc[:, tt * D:tt * D + D],
                                                  AF.Square, accum_out=stat[:, cb + 1:cb + 2]))
                        block.scalar(sc_sq5)
                        _s5 = sSC.n

                        def v_st5(v, tt=tt, cb=cb, _s5=_s5):
                            v.wait_ge(sV.h, _gfb)
                            if tt >= 2:
                                v.wait_ge(sV.h, outv[tt - 2])
                            sV.inc(v.reduce_sum(stat[:, cb:cb + 1], acc[:, tt * D:tt * D + D], axis=AX.X))
                            v.wait_ge(sSC.h, _s5)
                            v.wait_ge(sV.h, sV.n)
                            sV.inc(v.tensor_scalar(stat[:, cb + 2:cb + 3], stat[:, cb:cb + 1], 1.0 / D, None, ALU.mult))
                            sV.inc(v.tensor_scalar(stat[:, cb + 3:cb + 4], stat[:, cb + 1:cb + 2], 1.0 / D, None, ALU.mult))
                            v.wait_ge(sV.h, sV.n)
                            sV.inc(v.tensor_tensor(stat[:, cb + 4:cb + 5], stat[:, cb + 2:cb + 3],
                                                   stat[:, cb + 2:cb + 3], ALU.mult))
                            v.wait_ge(sV.h, sV.n)
                            sV.inc(v.tensor_sub(stat[:, cb + 5:cb + 6], stat[:, cb + 3:cb + 4],
                                                stat[:, cb + 4:cb + 5]))
                        block.vector(v_st5)
                        _v5 = sV.n

                        def sc_rt5(sc, cb=cb, _v5=_v5):
                            sc.wait_ge(sV.h, _v5)
                            sc.wait_ge(sG.h, 1)
                            sSC.inc(sc.activation(stat[:, cb + 6:cb + 7], stat[:, cb + 5:cb + 6],
                                                  AF.Sqrt, bias=epsc[:, 0:1]))
                        block.scalar(sc_rt5)
                        _r5 = sSC.n

                        def v_rs5(v, cb=cb, _r5=_r5):
                            v.wait_ge(sSC.h, _r5)
                            sV.inc(v.reciprocal(stat[:, cb + 7:cb + 8], stat[:, cb + 6:cb + 7]))
                        block.vector(v_rs5)
                        _n5 = sV.n

                        def v_x5(v, tt=tt, sl=sl, cb=cb, _n5=_n5):
                            v.wait_ge(sV.h, _n5)
                            sV.inc(v.tensor_scalar(xh5[:, sl:sl + D], acc[:, tt * D:tt * D + D],
                                                   stat[:, cb + 2:cb + 3], stat[:, cb + 7:cb + 8],
                                                   ALU.subtract, ALU.mult))
                        block.vector(v_x5)
                        _x5 = sV.n

                        def v_out(v, tt=tt, sl=sl, _x5=_x5):
                            v.wait_ge(sV.h, _x5)
                            W(v, x2l5[tt])
                            if tt >= 2:
                                W(v, out_dma[tt - 2])
                            sV.inc(v.tensor_mul(outst[:, sl:sl + D], xh5[:, sl:sl + D], GFB[:, :]))
                            v.wait_ge(sV.h, sV.n)
                            sV.inc(v.tensor_add(outst[:, sl:sl + D], outst[:, sl:sl + D], BFB[:, :]))
                            v.wait_ge(sV.h, sV.n)
                            sV.inc(v.tensor_add(outst[:, sl:sl + D], outst[:, sl:sl + D], x2r2[:, sl:sl + D]))
                            outv[tt] = sV.n
                        block.vector(v_out)

                        def sy_out(sy, tt=tt, sl=sl):
                            sy.wait_ge(sV.h, outv[tt])
                            sy.dma_start(y_out[OFF + tt * 128:OFF + tt * 128 + 128, :],
                                         outst[:, sl:sl + D]).then_inc(dOut[tt % 2].h, 16)
                            dOut[tt % 2].n += 16
                            out_dma[tt] = dOut[tt % 2].tok()
                        block.sync(sy_out)

                    fin = [(s.h, s.n) for s in ALL_DMA if s.n > 0]

                    def sy_fin(sy):
                        for tk in fin:
                            W(sy, tk)
                    block.sync(sy_fin)


    return nc



def kernel(L0_attn_x, L0_attn_W_qkv, L0_attn_ln_gamma, L0_attn_ln_beta,
           L0_attn_W_proj, L0_attn_b_proj, L0_ffn_Wg, L0_ffn_bg, L0_ffn_W1,
           L0_ffn_b1, L0_ffn_W2, L0_ffn_b2, L0_ffn_gamma, L0_ffn_beta,
           L0_c, L0_wmod):
    T = 2048
    f32 = np.float32
    x = np.ascontiguousarray(np.asarray(L0_attn_x, f32).reshape(8 * T, D))

    def bc(v, n=D):
        return np.ascontiguousarray(
            np.broadcast_to(np.asarray(v, f32).reshape(1, n), (128, n)))

    common = {
        "w_qkv": np.ascontiguousarray(np.asarray(L0_attn_W_qkv, f32)),
        "w_proj": np.ascontiguousarray(np.asarray(L0_attn_W_proj, f32)),
        "w_g": np.ascontiguousarray(np.asarray(L0_ffn_Wg, f32)),
        "w1": np.ascontiguousarray(np.asarray(L0_ffn_W1, f32)),
        "w2": np.ascontiguousarray(np.asarray(L0_ffn_W2, f32)),
        "b2_in": np.ascontiguousarray(np.asarray(L0_ffn_b2, f32)),
        "w_mod": np.ascontiguousarray(np.asarray(L0_wmod, f32)),
        "b1_cols": np.ascontiguousarray(
            np.asarray(L0_ffn_b1, f32).reshape(NE, 32, 128).transpose(2, 0, 1).reshape(128, NE * 32)),
        "gam_bc_in": bc(np.tile(np.asarray(L0_attn_ln_gamma, f32), NH)),
        "bet_bc_in": bc(np.tile(np.asarray(L0_attn_ln_beta, f32), NH)),
        "gf_bc_in": bc(L0_ffn_gamma),
        "bf_bc_in": bc(L0_ffn_beta),
        "bproj_bc_in": bc(L0_attn_b_proj),
        "bg_bc_in": bc(L0_ffn_bg, NE),
        "perm64_in": np.ascontiguousarray(np.roll(np.eye(128, dtype=f32), 64, axis=0)),
        "ident_in": np.eye(128, dtype=f32),
    }

    nc = build_nc(T)
    in_maps = []
    for core in range(8):
        m = dict(common)
        m["x_in"] = np.ascontiguousarray(x[core * T:(core + 1) * T])
        crow = np.asarray(L0_c, f32)[core // 4]
        m["c_col"] = np.ascontiguousarray(crow.reshape(NE, 128).T)
        in_maps.append(m)

    res = run_bass_kernel_spmd(nc, in_maps, core_ids=list(range(8)))
    out = np.concatenate([np.asarray(r["y_out"]) for r in res.results], axis=0)
    return np.ascontiguousarray(out.reshape(2, 8192, D))
